# version 02
# baseline (speedup 1.0000x reference)
import contextlib
import numpy as np


class _DbgFin(Exception):
    pass

import concourse.bass as bass
import concourse.mybir as mybir
from concourse.bass_utils import run_bass_kernel_spmd

F32 = mybir.dt.float32
F16 = mybir.dt.float16
BF16 = mybir.dt.bfloat16
AF = mybir.ActivationFunctionType
ALU = mybir.AluOpType
AXX = mybir.AxisListType.X

B, S, H = 2, 4096, 4096
NH, NKV, D = 32, 8, 128
HID = 11008
RANK = 16
EPS = 1e-5
QSCALE = float(2.0 ** -3.5)

NDT = 32
NCT = 32
NHT = 86
NRC = 8
ROWS = 1024
NRT = 8
NKT = (16, 32)
NDOT = 16
BIGMASK = 60000.0
SLOPES = [float(2.0 ** (-0.25 * (n + 1))) for n in range(NH)]
W2GRP = [11] * 7 + [9]

_DBG = {"stage": None}


def build_nc(dbg_stage=None):
    nc = bass.Bass("TRN2", target_bir_lowering=False, debug=False,
                   detect_race_conditions=False)
    es = contextlib.ExitStack()

    ap_in = lambda name, shape: nc.dram_tensor(name, shape, F32,
                                               kind="ExternalInput").ap()
    xt_in = ap_in("xt", [NRC, 128, NDT, 512])
    xto_in = ap_in("xto", [2, 128, NDT, 512])
    xown_in = ap_in("xown", [ROWS, H])
    wq_in = ap_in("wq", [32, 128, NDT, 128])
    wk_in = ap_in("wk", [8, 128, NDT, 128])
    wv_in = ap_in("wv", [NDT, 128, 1024])
    wo_in = ap_in("wo", [8, 128, NCT, 512])
    w1t_in = ap_in("w1t", [NHT, 128, NDT, 128])
    w3t_in = ap_in("w3t", [NHT, 128, NDT, 128])
    w2t_in = ap_in("w2t", [NDOT, NHT, 128, 256])
    a1t_in = ap_in("a1t", [128, NDT * RANK])
    a3t_in = ap_in("a3t", [128, NDT * RANK])
    a2t_in = ap_in("a2t", [128, NHT * RANK])
    b1t_in = ap_in("b1t", [RANK, HID])
    b3t_in = ap_in("b3t", [RANK, HID])
    b2t_in = ap_in("b2t", [RANK, H])
    rel_in = ap_in("rel", [sum(NKT), 128, 512])
    ident_in = ap_in("ident", [128, 128])

    out_ap = nc.dram_tensor("out", [ROWS, H], F32, kind="ExternalOutput").ap()
    dbg_ap = None
    if dbg_stage is not None:
        dbg_ap = nc.dram_tensor("dbg", [128, 98304], F32,
                                kind="ExternalOutput").ap()

    kt_dram = nc.dram_tensor("kt_dram", [NKV, 128, S], BF16).ap()
    v_dram = nc.dram_tensor("v_dram", [NKV, 128, 32 * 129], BF16).ap()
    ao_dram = nc.dram_tensor("ao_dram", [ROWS, H], BF16).ap()
    s1_dram = nc.dram_tensor("s1_dram", [ROWS, H], F32).ap()
    g_dram = nc.dram_tensor("g_dram", [NHT, 128, ROWS], BF16).ap()
    s2_dram = nc.dram_tensor("s2_dram", [ROWS, H], F32).ap()

    _guards = []

    def sbt(n, s, d):
        g = nc.sbuf_tensor(n, s, d)
        _guards.append(g)
        return g.__enter__()
    ident_bf = sbt("ident_bf", [128, 128], BF16)
    m1r1 = sbt("m1r1", [128, 2 * NRT], F32)
    m2r2 = sbt("m2r2", [128, 2 * NRT], F32)
    acc1 = sbt("acc1", [128, 2 * NRT], F32)
    acc2 = sbt("acc2", [128, 2 * NRT], F32)
    lnt1 = sbt("lnt1", [128, NRT], F32)
    lnt2 = sbt("lnt2", [128, NRT], F32)
    sqs = sbt("sqs", [128, 2 * 256], F32)
    a1t_sb = sbt("a1t_sb", [128, NDT * RANK], BF16)
    a3t_sb = sbt("a3t_sb", [128, NDT * RANK], BF16)
    a2t_sb = sbt("a2t_sb", [128, NHT * RANK], BF16)

    s_ld = es.enter_context(nc.semaphore("s_ld"))
    s_pl = es.enter_context(nc.semaphore("s_pl"))
    s_mm = es.enter_context(nc.semaphore("s_mm"))
    s_ev = es.enter_context(nc.semaphore("s_ev"))
    s_sc = es.enter_context(nc.semaphore("s_sc"))
    s_st = es.enter_context(nc.semaphore("s_st"))
    s_gs = es.enter_context(nc.semaphore("s_gs"))
    s_fn = es.enter_context(nc.semaphore("s_fn"))
    block = es.enter_context(nc.Block())

    C = {"ld": 0, "pl": 0, "mm": 0, "ev": 0, "sc": 0, "st": 0, "gs": 0,
         "fn": 0}
    SEM = {"ld": s_ld, "pl": s_pl, "mm": s_mm, "ev": s_ev, "sc": s_sc,
           "st": s_st, "gs": s_gs, "fn": s_fn}

    def inc(instr, key, n=1):
        instr.then_inc(SEM[key], n)
        C[key] += n
        return C[key]

    _streams = []

    class _PS:

        def __init__(self, name, per_group, nsem=2):
            self.nsem = nsem
            self.sems = []
            for i in range(nsem):
                gi = nc.semaphore(name + str(i))
                _guards.append(gi)
                self.sems.append(gi.__enter__())
            self.per = per_group
            self.n = 0
            self.cnt = [0] * nsem
            self._open = 0
            _streams.append(self)

        def inc(self, instr):
            s = self.n % self.nsem
            instr.then_inc(self.sems[s], 16)
            self.cnt[s] += 16
            self._open += 1
            if self._open == self.per:
                self._open = 0
                self.n += 1

        def wait_done(self, eng, g):
            if g < 0:
                return
            eng.wait_ge(self.sems[g % self.nsem],
                        16 * self.per * (g // self.nsem + 1))

        def wait_all(self, eng):
            for s in range(self.nsem):
                if self.cnt[s] > 0:
                    eng.wait_ge(self.sems[s], self.cnt[s])

    def fence():
        names = ("tensor", "vector", "scalar", "gpsimd", "sync")
        waits = [(k, C[k]) for k in ("ld", "pl", "mm", "ev", "sc", "st", "gs")]
        snaps = [(st_, tuple(st_.cnt)) for st_ in _streams]
        for name in names:
            @getattr(block, name)
            def _(E, w=tuple(waits), sn=tuple(snaps)):
                for k, val in w:
                    if val > 0:
                        E.wait_ge(SEM[k], val)
                for st_, cs in sn:
                    for i, ci in enumerate(cs):
                        if ci > 0:
                            E.wait_ge(st_.sems[i], ci)

    def dbg_dump(pairs, wait=None):
        @block.gpsimd
        def _(g):
            if wait is not None:
                g.wait_ge(SEM[wait[0]], wait[1])
            col = 0
            for src, n in pairs:
                dst = dbg_ap[:, col:col + n]
                if len(src.shape) == 3:
                    dst = dst.rearrange("p (r c) -> p r c", r=src.shape[1])
                inc(g.dma_start(out=dst, in_=src), "st", 16)
                col += n
        @block.sync
        def _(sy):
            sy.wait_ge(SEM["st"], C["st"])
        raise _DbgFin()

    try:
        @block.gpsimd
        def _(g):
            inc(g.dma_start(out=ident_bf[:, :], in_=ident_in), "ld", 16)
            inc(g.dma_start(out=a1t_sb[:, :], in_=a1t_in), "ld", 16)
            inc(g.dma_start(out=a3t_sb[:, :], in_=a3t_in), "ld", 16)
            inc(g.dma_start(out=a2t_sb[:, :], in_=a2t_in), "ld", 16)

        @block.vector
        def _(v):
            v.memset(acc1[:, :], 0.0)
            inc(v.memset(acc2[:, :], 0.0), "ev", 1)

        with (
            nc.sbuf_tensor("wk_sb", [128, 8 * NDT * 128], BF16) as wk_sb,
            nc.sbuf_tensor("wv_sb", [128, NDT * 1024], BF16) as wv_sb,
            nc.sbuf_tensor("xtc", [128, 2 * NDT * 512], BF16) as xtc,
            nc.sbuf_tensor("kev", [128, 2 * 512], BF16) as kev,
            nc.sbuf_tensor("vstage", [128, 2 * NKV * 129], BF16) as vstage,
            nc.psum_tensor("pk", [128, 2 * 512], F32) as pk,
            nc.psum_tensor("pv", [128, 2 * 1024], F32) as pv,
        ):
            ld0, mm0, ev0 = C["ld"], C["mm"], C["ev"]
            mmof = lambda gi: mm0 + gi + 1
            evof = lambda gi: ev0 + 1 + gi + 1
            ST_xtc = _PS("xtc_s", 1)
            ST_kst = _PS("kst_s", 1)
            ST_vst = _PS("vst_s", 1)

            @block.gpsimd
            def _(g):
                for o in range(8):
                    inc(g.dma_start(
                        out=wk_sb[:, o * NDT * 128:(o + 1) * NDT * 128],
                        in_=wk_in[o].rearrange("p a b -> p (a b)")), "ld", 16)
                inc(g.dma_start(
                    out=wv_sb[:, :].rearrange("p (a b) -> p a b", a=NDT),
                    in_=wv_in.rearrange("a p b -> p a b")), "ld", 16)
                for rc in range(NRC):
                    if rc >= 2:
                        g.wait_ge(SEM["mm"], mmof((rc - 2) * 12 + 11))
                    sl = (rc % 2) * NDT * 512
                    ST_xtc.inc(g.dma_start(
                        out=xtc[:, sl:sl + NDT * 512],
                        in_=xt_in[rc].rearrange("p a b -> p (a b)")))

            @block.vector
            def _(v):
                inc(v.memset(vstage[:, :], 1.0), "ev", 1)

            @block.tensor
            def _(t):
                t.wait_ge(SEM["ld"], ld0 + 16 * 9)
                for rc in range(NRC):
                    ST_xtc.wait_done(t, rc)
                    xb = (rc % 2) * NDT * 512
                    for kct in range(8):
                        gi = rc * 12 + kct
                        if gi >= 2:
                            t.wait_ge(SEM["ev"], evof(gi - 2))
                        ps = pk[:, (kct % 2) * 512:(kct % 2) * 512 + 512]
                        for dt in range(NDT):
                            ins = t.matmul(ps,
                                     wk_sb[:, (kct * NDT + dt) * 128:
                                           (kct * NDT + dt) * 128 + 128],
                                     xtc[:, xb + dt * 512:xb + dt * 512 + 512],
                                     start=(dt == 0), stop=(dt == NDT - 1),
                                     skip_group_check=True)
                        inc(ins, "mm", 1)
                    for rt in range(4):
                        gi = rc * 12 + 8 + rt
                        if gi >= 2:
                            t.wait_ge(SEM["ev"], evof(gi - 2))
                        pb = (rt % 2) * 1024
                        for dt in range(NDT):
                            lh = xtc[:, xb + dt * 512 + rt * 128:
                                     xb + dt * 512 + rt * 128 + 128]
                            for hf in range(2):
                                ins = t.matmul(
                                    pv[:, pb + hf * 512:pb + hf * 512 + 512],
                                    lh,
                                    wv_sb[:, dt * 1024 + hf * 512:
                                          dt * 1024 + hf * 512 + 512],
                                    start=(dt == 0), stop=(dt == NDT - 1),
                                    skip_group_check=True)
                        inc(ins, "mm", 1)

            @block.vector
            def _(v):
                for rc in range(NRC):
                    for kct in range(8):
                        gi = rc * 12 + kct
                        m = rc * 8 + kct
                        v.wait_ge(SEM["mm"], mmof(gi))
                        ST_kst.wait_done(v, m - 2)
                        sl = (kct % 2) * 512
                        inc(v.tensor_copy(kev[:, sl:sl + 512], pk[:, sl:sl + 512]),
                            "ev", 1)
                    for rt in range(4):
                        gi = rc * 12 + 8 + rt
                        n = rc * 4 + rt
                        v.wait_ge(SEM["mm"], mmof(gi))
                        ST_vst.wait_done(v, n - 2)
                        dsl = (rt % 2) * NKV * 129
                        inc(v.tensor_copy(
                            vstage[:, dsl:dsl + NKV * 129].rearrange(
                                "p (h c) -> p h c", h=NKV)[:, :, 0:128],
                            pv[:, (rt % 2) * 1024:(rt % 2) * 1024 + 1024].rearrange(
                                "p (h c) -> p h c", h=NKV)), "ev", 1)

            @block.sync
            def _(sy):
                for rc in range(NRC):
                    for kct in range(8):
                        gi = rc * 12 + kct
                        sy.wait_ge(SEM["ev"], evof(gi))
                        sl = (kct % 2) * 512
                        ST_kst.inc(sy.dma_start(
                            out=kt_dram[kct, :, rc * 512:rc * 512 + 512],
                            in_=kev[:, sl:sl + 512]))
                    for rt in range(4):
                        gi = rc * 12 + 8 + rt
                        sy.wait_ge(SEM["ev"], evof(gi))
                        dsl = (rt % 2) * NKV * 129
                        kt = rc * 4 + rt
                        ST_vst.inc(sy.dma_start(
                            out=v_dram[:, :, kt * 129:(kt + 1) * 129].rearrange(
                                "h p c -> p h c"),
                            in_=vstage[:, dsl:dsl + NKV * 129].rearrange(
                                "p (h c) -> p h c", h=NKV)))

        KV_STREAMS = (ST_kst, ST_vst)
        fence()
        if dbg_stage == "kv":
            dbg_dump([(kt_dram[1], S), (v_dram[1], 32 * 129)])

        with nc.sbuf_tensor("qt_sb", [128, NCT * ROWS], BF16) as qt_sb:
            with (
                nc.sbuf_tensor("xto_sb", [128, NDT * ROWS], BF16) as xto_sb,
                nc.sbuf_tensor("wq_sb", [128, 2 * NDT * 128], BF16) as wq_sb,
                nc.psum_tensor("pq", [128, 4 * 512], F32) as pq,
            ):
                ld0, mm0, ev0 = C["ld"], C["mm"], C["ev"]
                ST_wq = _PS("wq_s", 1)

                @block.gpsimd
                def _(g):
                    for qc in range(2):
                        inc(g.dma_start(
                            out=xto_sb[:, :].rearrange(
                                "p (a q c) -> p a q c", a=NDT, q=2)[:, :, qc, :],
                            in_=xto_in[qc]), "ld", 16)
                    for oc in range(32):
                        if oc >= 2:
                            g.wait_ge(SEM["mm"], mm0 + 2 * (oc - 1))
                        ST_wq.inc(g.dma_start(
                            out=wq_sb[:, (oc % 2) * NDT * 128:
                                      (oc % 2 + 1) * NDT * 128],
                            in_=wq_in[oc].rearrange("p a b -> p (a b)")))

                @block.tensor
                def _(t):
                    t.wait_ge(SEM["ld"], ld0 + 32)
                    for oc in range(32):
                        ST_wq.wait_done(t, oc)
                        wb = (oc % 2) * NDT * 128
                        for rc in range(2):
                            gi = oc * 2 + rc
                            if gi >= 4:
                                t.wait_ge(SEM["ev"], ev0 + gi - 3)
                            ps = pq[:, (gi % 4) * 512:(gi % 4) * 512 + 512]
                            for dt in range(NDT):
                                ins = t.matmul(ps,
                                         wq_sb[:, wb + dt * 128:wb + dt * 128 + 128],
                                         xto_sb[:, dt * 1024 + rc * 512:
                                                dt * 1024 + rc * 512 + 512],
                                         start=(dt == 0), stop=(dt == NDT - 1),
                                         skip_group_check=True)
                            inc(ins, "mm", 1)

                @block.vector
                def _(v):
                    for gi in range(64):
                        v.wait_ge(SEM["mm"], mm0 + gi + 1)
                        oc, rc = divmod(gi, 2)
                        ps = pq[:, (gi % 4) * 512:(gi % 4) * 512 + 512]
                        inc(v.tensor_scalar_mul(
                            qt_sb[:, oc * ROWS + rc * 512:oc * ROWS + rc * 512 + 512],
                            ps, QSCALE), "ev", 1)

            fence()
            if dbg_stage == "q":
                dbg_dump([(qt_sb[:, 0:NCT * ROWS], NCT * ROWS)])

            with (
                nc.sbuf_tensor("biasq", [128, 32 * 512], F16) as biasq,
                nc.sbuf_tensor("relbuf", [128, 2 * 512], F32) as relbuf,
                nc.sbuf_tensor("relmax", [128, 2 * 512], F32) as relmax,
                nc.sbuf_tensor("ktbuf", [128, 2 * 32 * 128], BF16) as ktbuf,
                nc.sbuf_tensor("vbuf", [128, 2 * 32 * 129], BF16) as vbuf,
                nc.sbuf_tensor("ssb", [128, 2 * 512], F32) as ssb,
                nc.sbuf_tensor("pbuf", [128, 2 * 512], BF16) as pbuf,
                nc.sbuf_tensor("rscol", [128, 4], F32) as rscol,
                nc.sbuf_tensor("aop", [128, 2 * 512], BF16) as aop,
                nc.psum_tensor("ps_s", [128, 2 * 512], F32) as ps_s,
                nc.psum_tensor("ps_av", [128, 4 * 512], F32) as ps_av,
            ):
                ST_rel = _PS("rel_s", 1)
                ST_kv = _PS("kv_s", 2)
                ST_ao = _PS("ao_s", 4)
                for qc in range(2):
                    nkt = NKT[qc]
                    dbg_av = (dbg_stage == "av0" and qc == 0)
                    DBGU = 3
                    NH_EFF = (DBGU + 1) if dbg_av else NH
                    KV_EFF = 1 if dbg_av else NKV
                    rel0 = 0 if qc == 0 else NKT[0]
                    relb = ST_rel.n
                    kvb = ST_kv.n
                    aob = ST_ao.n
                    pl0, ev0, sc0, mm0, gs0 = (C["pl"], C["ev"], C["sc"],
                                               C["mm"], C["gs"])

                    order = [("s", 0), ("s", 1)]
                    for k in range(2, nkt):
                        order += [("s", k), ("av", k - 2)]
                    order += [("av", nkt - 2), ("av", nkt - 1)]
                    POS = {e: i for i, e in enumerate(order)}

                    def MMc(u, kind, kt, _P=POS, _b=mm0, _n=2 * nkt):
                        return _b + u * _n + _P[(kind, kt)] + 1

                    evb = ev0 + 2 * nkt

                    def EVstt(u, kt, _b=evb, _n=nkt):
                        return _b + u * (_n + 4) + kt + 1

                    def EVav(u, qt, _b=evb, _n=nkt):
                        return _b + u * (_n + 4) + _n + qt + 1

                    scb = sc0 + nkt

                    def SCexp(u, kt, _b=scb, _n=nkt):
                        return _b + u * _n + kt + 1



                    @block.gpsimd
                    def _(g, nkt=nkt, rel0=rel0, ev0=ev0):
                        for kt in range(nkt):
                            if kt >= 2:
                                g.wait_ge(SEM["ev"], ev0 + 2 * (kt - 1))
                            ST_rel.inc(g.dma_start(
                                out=relbuf[:, (kt % 2) * 512:(kt % 2) * 512 + 512],
                                in_=rel_in[rel0 + kt]))

                    @block.vector
                    def _(v, nkt=nkt, relb=relb, sc0=sc0):
                        for kt in range(nkt):
                            ST_rel.wait_done(v, relb + kt)
                            sl = (kt % 2) * 512
                            inc(v.tensor_scalar_max(relmax[:, sl:sl + 512],
                                                     relbuf[:, sl:sl + 512], 0.0),
                                "ev", 1)
                            v.wait_ge(SEM["sc"], sc0 + kt + 1)
                            v.tensor_single_scalar(relmax[:, sl:sl + 512],
                                                   relbuf[:, sl:sl + 512], 0.0,
                                                   ALU.is_lt)
                            inc(v.scalar_tensor_tensor(
                                biasq[:, kt * 512:kt * 512 + 512],
                                relmax[:, sl:sl + 512], BIGMASK,
                                biasq[:, kt * 512:kt * 512 + 512],
                                ALU.mult, ALU.add), "ev", 1)

                    @block.scalar
                    def _(s, nkt=nkt, ev0=ev0):
                        for kt in range(nkt):
                            s.wait_ge(SEM["ev"], ev0 + 2 * kt + 1)
                            sl = (kt % 2) * 512
                            inc(s.activation(biasq[:, kt * 512:kt * 512 + 512],
                                             relmax[:, sl:sl + 512], AF.Sqrt),
                                "sc", 1)

                    if dbg_stage == "bias" and qc == 0:
                        dbg_dump([(biasq[:, 0:16 * 512], 16 * 512),
                                  (relbuf[:, 0:1024], 1024),
                                  (relmax[:, 0:1024], 1024)],
                                 wait=("ev", evb))

                    @block.sync
                    def _(sy, nkt=nkt, qc=qc, MMc=MMc):
                        if qc == 0:
                            ST_kst.wait_all(sy)
                            ST_vst.wait_all(sy)
                        for h in range(KV_EFF):
                            if h >= 2:
                                sy.wait_ge(SEM["mm"],
                                           MMc((h - 2) * 4 + 3, "av", nkt - 1))
                            ksl = (h % 2) * 32 * 128
                            vsl = (h % 2) * 32 * 129
                            ST_kv.inc(sy.dma_start(
                                out=ktbuf[:, ksl:ksl + nkt * 128],
                                in_=kt_dram[h, :, 0:nkt * 128]))
                            ST_kv.inc(sy.dma_start(
                                out=vbuf[:, vsl:vsl + nkt * 129],
                                in_=v_dram[h, :, 0:nkt * 129]))

                    @block.gpsimd
                    def _(g, nkt=nkt, qc=qc, EVav=EVav, aob=aob):
                        for u in range(DBGU if dbg_av else NH_EFF):
                            for qt in range(4):
                                g.wait_ge(SEM["ev"], EVav(u, qt))
                                ST_ao.inc(g.dma_start(
                                    out=ao_dram[qc * 512 + qt * 128:
                                                qc * 512 + qt * 128 + 128,
                                                u * 128:u * 128 + 128],
                                    in_=aop[:, (u % 2) * 512 + qt * 128:
                                            (u % 2) * 512 + qt * 128 + 128]))

                    @block.tensor
                    def _(t, nkt=nkt, qc=qc, order=order, MMc=MMc, EVstt=EVstt,
                          EVav=EVav, SCexp=SCexp, kvb=kvb):
                        for h in range(KV_EFF):
                            ST_kv.wait_done(t, kvb + h)
                            ksl = (h % 2) * 32 * 128
                            vsl = (h % 2) * 32 * 129
                            for gg in range(4 if not dbg_av else NH_EFF):
                                u = h * 4 + gg
                                for kind, kt in order:
                                    if kind == "s":
                                        i = u * nkt + kt - 2
                                        if i >= 0:
                                            pu_, pkt = divmod(i, nkt)
                                            t.wait_ge(SEM["ev"], EVstt(pu_, pkt))
                                        inc(t.matmul(
                                            ps_s[:, (kt % 2) * 512:
                                                 (kt % 2) * 512 + 512],
                                            ktbuf[:, ksl + kt * 128:
                                                  ksl + kt * 128 + 128],
                                            qt_sb[:, u * ROWS + qc * 512:
                                                  u * ROWS + qc * 512 + 512],
                                            start=True, stop=True,
                                            skip_group_check=True), "mm", 1)
                                    else:
                                        t.wait_ge(SEM["sc"], SCexp(u, kt))
                                        if kt == 0 and u > 0:
                                            t.wait_ge(SEM["ev"], EVav(u - 1, 3))
                                        for qt in range(4):
                                            ins = t.matmul(
                                                ps_av[:, qt * 512:qt * 512 + 129],
                                                pbuf[:, (kt % 2) * 512 + qt * 128:
                                                     (kt % 2) * 512 + qt * 128 + 128],
                                                vbuf[:, vsl + kt * 129:
                                                     vsl + kt * 129 + 129],
                                                start=(kt == 0),
                                                stop=(kt == nkt - 1),
                                                skip_group_check=True)
                                        inc(ins, "mm", 1)

                    @block.scalar
                    def _(s, nkt=nkt, EVstt=EVstt, MMc=MMc):
                        for u in range(NH_EFF):
                            for kt in range(nkt):
                                s.wait_ge(SEM["ev"], EVstt(u, kt))
                                i = u * nkt + kt - 2
                                if i >= 0:
                                    pu_, pkt = divmod(i, nkt)
                                    s.wait_ge(SEM["mm"], MMc(pu_, "av", pkt))
                                sl = (kt % 2) * 512
                                inc(s.activation(pbuf[:, sl:sl + 512],
                                                 ssb[:, sl:sl + 512], AF.Exp),
                                    "sc", 1)

                    if dbg_stage is None:
                        pass
                    if dbg_av:
                        dbgsb = nc.sbuf_tensor("dbgsb", [128, 8192], F32).__enter__()
                        _guards.append(dbgsb)

                    @block.vector
                    def _(v, nkt=nkt, MMc=MMc, SCexp=SCexp, aob=aob):
                        for u in range(NH_EFF):
                            slope = SLOPES[u]
                            for kt in range(nkt):
                                v.wait_ge(SEM["mm"], MMc(u, "s", kt))
                                i = u * nkt + kt - 2
                                if i >= 0:
                                    pu_, pkt = divmod(i, nkt)
                                    v.wait_ge(SEM["sc"], SCexp(pu_, pkt))
                                sl = (kt % 2) * 512
                                inc(v.scalar_tensor_tensor(
                                    ssb[:, sl:sl + 512],
                                    biasq[:, kt * 512:kt * 512 + 512], -slope,
                                    ps_s[:, sl:sl + 512], ALU.mult, ALU.add),
                                    "ev", 1)
                            v.wait_ge(SEM["mm"], MMc(u, "av", nkt - 1))
                            ST_ao.wait_done(v, aob + u - 2)
                            if dbg_av and u == DBGU:
                                v.tensor_copy(dbgsb[:, 0:2048], ps_av[:, :])
                                v.tensor_copy(dbgsb[:, 2048:2048 + 2064],
                                              vbuf[:, 0:2064])
                                v.tensor_copy(dbgsb[:, 4112:4112 + 2048],
                                              ktbuf[:, 0:2048])
                                inc(v.tensor_copy(dbgsb[:, 6160:6160 + 512],
                                                  pbuf[:, 0:512]), "ev", 1)
                                dbg_dump([(dbgsb[:, 0:8192], 8192)],
                                         wait=("ev", C["ev"]))
                            ab = (u % 2) * 512
                            for qt in range(4):
                                v.reciprocal(rscol[:, qt:qt + 1],
                                             ps_av[:, qt * 512 + 128:qt * 512 + 129])
                            for qt in range(4):
                                inc(v.tensor_scalar_mul(
                                    aop[:, ab + qt * 128:ab + qt * 128 + 128],
                                    ps_av[:, qt * 512:qt * 512 + 128],
                                    rscol[:, qt:qt + 1]), "ev", 1)

                    fence()

        P2_GS = C["gs"]
        if dbg_stage == "attn":
            dbg_dump([(ao_dram.rearrange("(r p) c -> p r c", p=128), 8 * H)])

        with (
            nc.sbuf_tensor("aot_sb", [128, NCT * ROWS], BF16) as aot_sb,
            nc.sbuf_tensor("aotm", [128, 4 * 128], BF16) as aotm,
            nc.sbuf_tensor("wobuf", [128, 2 * NCT * 512], BF16) as wobuf,
            nc.sbuf_tensor("xpb", [128, 4 * 512], F32) as xpb,
            nc.sbuf_tensor("s1pb", [128, 4 * 512], F32) as s1pb,
            nc.sbuf_tensor("sqs3", [128, 2 * 512], F32) as sqs3,
            nc.psum_tensor("ps_tr", [128, 4 * 1024], BF16) as ps_tr,
            nc.psum_tensor("po3", [128, 4 * 512], F32) as po3,
        ):
            pl0, ev0, mm0 = C["pl"], C["ev"], C["mm"]
            ST_atm = _PS("atm_s", 2)
            ST_wo = _PS("wo_s", 1)
            ST_xp = _PS("xp_s", 2)
            ST_s1w = _PS("s1w_s", 2)

            @block.sync
            def _(sy):
                ST_ao.wait_all(sy)
                for i in range(256):
                    ct, rt = divmod(i, NRT)
                    if i >= 4:
                        sy.wait_ge(SEM["mm"], mm0 + i - 3)
                    ST_atm.inc(sy.dma_start(
                        out=aotm[:, (i % 4) * 128:(i % 4) * 128 + 128],
                        in_=ao_dram[rt * 128:rt * 128 + 128,
                                    ct * 128:ct * 128 + 128]))

            @block.tensor
            def _(t):
                for i in range(256):
                    ST_atm.wait_done(t, i // 2)
                    if i >= 4:
                        t.wait_ge(SEM["ev"], ev0 + i - 3)
                    inc(t.transpose(ps_tr[:, (i % 4) * 1024:(i % 4) * 1024 + 128],
                                    aotm[:, (i % 4) * 128:(i % 4) * 128 + 128],
                                    ident_bf[:, :]), "mm", 1)

            @block.vector
            def _(v):
                for i in range(256):
                    ct, rt = divmod(i, NRT)
                    v.wait_ge(SEM["mm"], mm0 + i + 1)
                    inc(v.tensor_copy(
                        aot_sb[:, ct * ROWS + rt * 128:ct * ROWS + rt * 128 + 128],
                        ps_tr[:, (i % 4) * 1024:(i % 4) * 1024 + 128]), "ev", 1)

            trn_mm = mm0 + 256
            trn_ev = ev0 + 256
            ld0, sc0 = C["ld"], C["sc"]
            evb = trn_ev

            def EVs1(i2, _b=evb):
                return _b + 2 * i2 + 1

            def EVst(i2, _b=evb):
                return _b + 2 * i2 + 2

            def SCsq(i2, _b=sc0):
                return _b + i2 + 1

            def MMwo(i2, _b=trn_mm):
                return _b + i2 + 1

            @block.gpsimd
            def _(g):
                def _load(hs):
                    if hs >= 2:
                        g.wait_ge(SEM["mm"], MMwo((hs - 1) * NRT - 1))
                    ST_wo.inc(g.dma_start(
                        out=wobuf[:, (hs % 2) * NCT * 512:(hs % 2 + 1) * NCT * 512],
                        in_=wo_in[hs].rearrange("p a b -> p (a b)")))
                _load(0)
                _load(1)
                for i2 in range(64):
                    hs, rt = divmod(i2, NRT)
                    if i2 % NRT == 0 and hs + 2 < 8:
                        _load(hs + 2)
                    g.wait_ge(SEM["ev"], EVst(i2))
                    sl = (i2 % 4) * 512
                    ST_s1w.inc(g.dma_start(
                        out=s1_dram[rt * 128:rt * 128 + 128,
                                    hs * 512:hs * 512 + 512],
                        in_=s1pb[:, sl:sl + 512]))

            @block.sync
            def _(sy):
                for i2 in range(64):
                    hs, rt = divmod(i2, NRT)
                    if i2 >= 4:
                        sy.wait_ge(SEM["ev"], EVs1(i2 - 4))
                    ST_xp.inc(sy.dma_start(
                        out=xpb[:, (i2 % 4) * 512:(i2 % 4) * 512 + 512],
                        in_=xown_in[rt * 128:rt * 128 + 128,
                                    hs * 512:hs * 512 + 512]))

            @block.tensor
            def _(t):
                t.wait_ge(SEM["ev"], trn_ev)
                for i2 in range(64):
                    hs, rt = divmod(i2, NRT)
                    ST_wo.wait_done(t, hs)
                    if i2 >= 4:
                        t.wait_ge(SEM["ev"], EVs1(i2 - 4))
                    wb = (hs % 2) * NCT * 512
                    ps = po3[:, (i2 % 4) * 512:(i2 % 4) * 512 + 512]
                    for ct in range(NCT):
                        ins = t.matmul(ps,
                                 aot_sb[:, ct * ROWS + rt * 128:
                                        ct * ROWS + rt * 128 + 128],
                                 wobuf[:, wb + ct * 512:wb + ct * 512 + 512],
                                 start=(ct == 0), stop=(ct == NCT - 1),
                                 skip_group_check=True)
                    inc(ins, "mm", 1)

            @block.vector
            def _(v):
                for i2 in range(64):
                    hs, rt = divmod(i2, NRT)
                    v.wait_ge(SEM["mm"], MMwo(i2))
                    ST_xp.wait_done(v, i2 // 2)
                    ST_s1w.wait_done(v, (i2 - 4) // 2 if i2 >= 4 else -1)
                    sl = (i2 % 4) * 512
                    inc(v.tensor_add(s1pb[:, sl:sl + 512], po3[:, sl:sl + 512],
                                     xpb[:, sl:sl + 512]), "ev", 1)
                    v.reduce_sum(lnt1[:, rt:rt + 1], s1pb[:, sl:sl + 512], AXX)
                    v.tensor_add(acc1[:, rt:rt + 1], acc1[:, rt:rt + 1],
                                 lnt1[:, rt:rt + 1])
                    v.wait_ge(SEM["sc"], SCsq(i2))
                    inc(v.tensor_add(acc1[:, NRT + rt:NRT + rt + 1],
                                     acc1[:, NRT + rt:NRT + rt + 1],
                                     lnt2[:, rt:rt + 1]), "ev", 1)

            @block.scalar
            def _(s):
                for i2 in range(64):
                    hs, rt = divmod(i2, NRT)
                    s.wait_ge(SEM["ev"], EVs1(i2))
                    sl = (i2 % 4) * 512
                    inc(s.activation(sqs3[:, 0:512], s1pb[:, sl:sl + 512],
                                     AF.Square, accum_out=lnt2[:, rt:rt + 1]),
                        "sc", 1)

        fence()
        if dbg_stage == "s1":
            dbg_dump([(s1_dram.rearrange("(r p) c -> p r c", p=128), 8 * H)])

        ev0, sc0 = C["ev"], C["sc"]

        @block.vector
        def _(v):
            v.tensor_scalar_mul(m1r1[:, 0:NRT], acc1[:, 0:NRT], 1.0 / H)
            v.tensor_scalar_mul(lnt1[:, :], acc1[:, NRT:2 * NRT], 1.0 / H)
            v.tensor_mul(lnt2[:, :], m1r1[:, 0:NRT], m1r1[:, 0:NRT])
            v.tensor_sub(lnt1[:, :], lnt1[:, :], lnt2[:, :])
            inc(v.tensor_scalar_add(lnt1[:, :], lnt1[:, :], EPS), "ev", 1)
            v.wait_ge(SEM["sc"], sc0 + 1)
            inc(v.reciprocal(m1r1[:, NRT:2 * NRT], lnt2[:, :]), "ev", 1)

        @block.scalar
        def _(s):
            s.wait_ge(SEM["ev"], ev0 + 1)
            inc(s.activation(lnt2[:, :], lnt1[:, :], AF.Sqrt), "sc", 1)

        ln1_ev = C["ev"]

        with nc.sbuf_tensor("yt_sb", [128, NDT * ROWS], BF16) as yt_sb:
            with (
                nc.sbuf_tensor("srow", [128, 2 * H], F32) as srow,
                nc.sbuf_tensor("ybf", [128, 2 * H], BF16) as ybf,
                nc.psum_tensor("ps_tr4", [128, 4 * 1024], BF16) as ps_tr4,
            ):
                pl0, ev0, mm0 = C["pl"], C["ev"], C["mm"]
                ST_sr = _PS("sr_s", 1)

                def EVy(rt, _b=ev0):
                    return _b + rt * 33 + 1

                def EVcp(rt, dt, _b=ev0):
                    return _b + rt * 33 + dt + 2

                def MMtr(rt, dt, _b=mm0):
                    return _b + rt * 32 + dt + 1

                @block.sync
                def _(sy):
                    ST_s1w.wait_all(sy)
                    for rt in range(NRT):
                        if rt >= 2:
                            sy.wait_ge(SEM["ev"], EVy(rt - 1))
                        ST_sr.inc(sy.dma_start(
                            out=srow[:, (rt % 2) * H:(rt % 2) * H + H],
                            in_=s1_dram[rt * 128:rt * 128 + 128, :]))

                @block.vector
                def _(v):
                    v.wait_ge(SEM["ev"], ln1_ev)
                    for rt in range(NRT):
                        ST_sr.wait_done(v, rt)
                        sl = (rt % 2) * H
                        inc(v.tensor_scalar(ybf[:, sl:sl + H], srow[:, sl:sl + H],
                                            m1r1[:, rt:rt + 1],
                                            m1r1[:, NRT + rt:NRT + rt + 1],
                                            ALU.subtract, ALU.mult), "ev", 1)
                        for dt in range(NDT):
                            v.wait_ge(SEM["mm"], MMtr(rt, dt))
                            inc(v.tensor_copy(
                                yt_sb[:, dt * ROWS + rt * 128:
                                      dt * ROWS + rt * 128 + 128],
                                ps_tr4[:, (dt % 4) * 1024:(dt % 4) * 1024 + 128]),
                                "ev", 1)

                @block.tensor
                def _(t):
                    for rt in range(NRT):
                        t.wait_ge(SEM["ev"], EVy(rt))
                        sl = (rt % 2) * H
                        for dt in range(NDT):
                            if rt * 32 + dt >= 4:
                                prt, pdt = divmod(rt * 32 + dt - 4, 32)
                                t.wait_ge(SEM["ev"], EVcp(prt, pdt))
                            inc(t.transpose(
                                ps_tr4[:, (dt % 4) * 1024:(dt % 4) * 1024 + 128],
                                ybf[:, sl + dt * 128:sl + dt * 128 + 128],
                                ident_bf[:, :]), "mm", 1)

            fence()
            if dbg_stage == "y":
                dbg_dump([(yt_sb[:, 0:NDT * ROWS], NDT * ROWS)])

            with (
                nc.sbuf_tensor("t1pad", [128, ROWS], BF16) as t1pad,
                nc.sbuf_tensor("t3pad", [128, ROWS], BF16) as t3pad,
                nc.sbuf_tensor("b1pad", [128, HID], BF16) as b1pad,
                nc.sbuf_tensor("b3pad", [128, HID], BF16) as b3pad,
                nc.sbuf_tensor("w1buf", [128, 3 * NDT * 128], BF16) as w1buf,
                nc.sbuf_tensor("w3buf", [128, 3 * NDT * 128], BF16) as w3buf,
                nc.sbuf_tensor("tsil", [128, 2 * 1024], F32) as tsil,
                nc.sbuf_tensor("gst", [128, 2 * 1024], BF16) as gst,
            ):
                ld0, ev0, mm0, sc0 = C["ld"], C["ev"], C["mm"], C["sc"]
                ST_w13 = _PS("w13_s", 2, nsem=3)
                ST_g5 = _PS("g5_s", 1)
                MMt1 = mm0 + 1

                def MMh1(ht, _b=mm0):
                    return _b + 1 + 2 * ht + 1

                def MMh3(ht, _b=mm0):
                    return _b + 1 + 2 * ht + 2

                def EVmul(ht, _b=ev0):
                    return _b + 2 + ht + 1

                def SCsil(ht, _b=sc0):
                    return _b + ht + 1

                @block.vector
                def _(v):
                    v.memset(t1pad[:, :], 0.0)
                    v.memset(t3pad[:, :], 0.0)
                    v.memset(b1pad[:, :], 0.0)
                    inc(v.memset(b3pad[:, :], 0.0), "ev", 1)

                @block.gpsimd
                def _(g):
                    g.wait_ge(SEM["ev"], ev0 + 1)
                    inc(g.dma_start(out=b1pad[0:RANK, :], in_=b1t_in), "ld", 16)
                    inc(g.dma_start(out=b3pad[0:RANK, :], in_=b3t_in), "ld", 16)
                    for ht in range(NHT):
                        if ht >= 3:
                            g.wait_ge(SEM["mm"], MMh3(ht - 3))
                        ST_w13.inc(g.dma_start(
                            out=w1buf[:, (ht % 3) * NDT * 128:
                                      (ht % 3 + 1) * NDT * 128],
                            in_=w1t_in[ht].rearrange("p a b -> p (a b)")))
                        ST_w13.inc(g.dma_start(
                            out=w3buf[:, (ht % 3) * NDT * 128:
                                      (ht % 3 + 1) * NDT * 128],
                            in_=w3t_in[ht].rearrange("p a b -> p (a b)")))

                with nc.psum_tensor("pt15", [128, 2 * 1024], F32) as pt15:
                    @block.tensor
                    def _(t):
                        t.wait_ge(SEM["ev"], ev0 + 1)
                        for half in range(2):
                            for dt in range(NDT):
                                t.matmul(pt15[0:RANK, half * 512:half * 512 + 512],
                                         a1t_sb[:, dt * RANK:dt * RANK + RANK],
                                         yt_sb[:, dt * ROWS + half * 512:
                                               dt * ROWS + half * 512 + 512],
                                         start=(dt == 0), stop=(dt == NDT - 1),
                                         skip_group_check=True)
                            for dt in range(NDT):
                                ins = t.matmul(pt15[0:RANK, 1024 + half * 512:
                                               1024 + half * 512 + 512],
                                         a3t_sb[:, dt * RANK:dt * RANK + RANK],
                                         yt_sb[:, dt * ROWS + half * 512:
                                               dt * ROWS + half * 512 + 512],
                                         start=(dt == 0), stop=(dt == NDT - 1),
                                         skip_group_check=True)
                        inc(ins, "mm", 1)

                    @block.vector
                    def _(v):
                        v.wait_ge(SEM["mm"], MMt1)
                        v.tensor_scalar_mul(t1pad[0:RANK, :], pt15[0:RANK, 0:1024],
                                            2.0)
                        inc(v.tensor_scalar_mul(t3pad[0:RANK, :],
                                                pt15[0:RANK, 1024:2048], 2.0),
                            "ev", 1)

                with (
                    nc.psum_tensor("ph1", [128, 2 * 1024], F32) as ph1,
                    nc.psum_tensor("ph3", [128, 2 * 1024], F32) as ph3,
                ):
                    @block.tensor
                    def _(t):
                        t.wait_ge(SEM["ev"], ev0 + 2)
                        t.wait_ge(SEM["ld"], ld0 + 32)
                        for ht in range(NHT):
                            ST_w13.wait_done(t, ht)
                            if ht >= 2:
                                t.wait_ge(SEM["sc"], SCsil(ht - 2))
                                t.wait_ge(SEM["ev"], EVmul(ht - 2))
                            wb = (ht % 3) * NDT * 128
                            pb = (ht % 2) * 1024
                            for dt in range(NDT):
                                for hf in range(2):
                                    t.matmul(
                                        ph1[:, pb + hf * 512:pb + hf * 512 + 512],
                                        w1buf[:, wb + dt * 128:wb + dt * 128 + 128],
                                        yt_sb[:, dt * ROWS + hf * 512:
                                              dt * ROWS + hf * 512 + 512],
                                        start=(dt == 0), stop=False,
                                        skip_group_check=True)
                            for hf in range(2):
                                ins = t.matmul(
                                    ph1[:, pb + hf * 512:pb + hf * 512 + 512],
                                    b1pad[:, ht * 128:ht * 128 + 128],
                                    t1pad[:, hf * 512:hf * 512 + 512],
                                    start=False, stop=True,
                                    skip_group_check=True)
                            inc(ins, "mm", 1)
                            for dt in range(NDT):
                                for hf in range(2):
                                    t.matmul(
                                        ph3[:, pb + hf * 512:pb + hf * 512 + 512],
                                        w3buf[:, wb + dt * 128:wb + dt * 128 + 128],
                                        yt_sb[:, dt * ROWS + hf * 512:
                                              dt * ROWS + hf * 512 + 512],
                                        start=(dt == 0), stop=False,
                                        skip_group_check=True)
                            for hf in range(2):
                                ins = t.matmul(
                                    ph3[:, pb + hf * 512:pb + hf * 512 + 512],
                                    b3pad[:, ht * 128:ht * 128 + 128],
                                    t3pad[:, hf * 512:hf * 512 + 512],
                                    start=False, stop=True,
                                    skip_group_check=True)
                            inc(ins, "mm", 1)

                    @block.scalar
                    def _(s):
                        for ht in range(NHT):
                            s.wait_ge(SEM["mm"], MMh1(ht))
                            pb = (ht % 2) * 1024
                            inc(s.activation(tsil[:, pb:pb + 1024],
                                             ph1[:, pb:pb + 1024], AF.Silu),
                                "sc", 1)

                    @block.vector
                    def _(v):
                        for ht in range(NHT):
                            v.wait_ge(SEM["sc"], SCsil(ht))
                            v.wait_ge(SEM["mm"], MMh3(ht))
                            ST_g5.wait_done(v, ht - 2)
                            pb = (ht % 2) * 1024
                            inc(v.tensor_mul(gst[:, pb:pb + 1024],
                                             tsil[:, pb:pb + 1024],
                                             ph3[:, pb:pb + 1024]), "ev", 1)

                    @block.sync
                    def _(sy):
                        for ht in range(NHT):
                            sy.wait_ge(SEM["ev"], EVmul(ht))
                            pb = (ht % 2) * 1024
                            ST_g5.inc(sy.dma_start(out=g_dram[ht],
                                                   in_=gst[:, pb:pb + 1024]))

            fence()
        if dbg_stage == "g":
            dbg_dump([(g_dram[i].rearrange("p b -> p b"), ROWS)
                      for i in range(4)] +
                     [(g_dram[40 + i], ROWS) for i in range(4)])

        with (
            nc.sbuf_tensor("gbuf", [128, NHT * ROWS], BF16) as gbuf,
            nc.sbuf_tensor("w2buf", [128, 2 * 11 * 256], BF16) as w2buf,
            nc.sbuf_tensor("t2pad", [128, ROWS], BF16) as t2pad,
            nc.sbuf_tensor("b2pad", [128, 2 * 256], BF16) as b2pad,
            nc.sbuf_tensor("s1qb", [128, 8 * 256], F32) as s1qb,
        ):
            pl0, ld0, ev0, mm0, sc0 = (C["pl"], C["ld"], C["ev"], C["mm"],
                                       C["sc"])
            ST_w2 = _PS("w2_s", 1)
            ST_b2 = _PS("b2_s", 1)
            ST_s1q = _PS("s1q_s", 8)
            ST_s2w = _PS("s2w_s", 8)

            def EVyp(dot, qt, _b=ev0):
                return _b + 2 + dot * 24 + qt + 1

            def EVs2(dot, qt, _b=ev0):
                return _b + 2 + dot * 24 + 8 + 2 * qt + 1

            def EVst2(dot, qt, _b=ev0):
                return _b + 2 + dot * 24 + 8 + 2 * qt + 2

            def MMgrp(gi, _b=mm0):
                dot_, grp_ = divmod(gi, 8)
                return _b + 1 + dot_ * 16 + grp_ + 1

            def MMq(dot, qt, _b=mm0):
                return _b + 1 + dot * 16 + 8 + qt + 1

            def SCsq2(i3, _b=sc0):
                return _b + i3 + 1

            @block.vector
            def _(v):
                v.memset(t2pad[:, :], 0.0)
                inc(v.memset(b2pad[:, :], 0.0), "ev", 1)

            @block.gpsimd
            def _(g):
                g.wait_ge(SEM["ev"], ev0 + 1)
                gi = 0
                for dot in range(NDOT):
                    if dot >= 2:
                        g.wait_ge(SEM["mm"], MMq(dot - 2, 7))
                    ST_b2.inc(g.dma_start(
                        out=b2pad[0:RANK, (dot % 2) * 256:(dot % 2) * 256 + 256],
                        in_=b2t_in[:, dot * 256:dot * 256 + 256]))
                    h0 = 0
                    for grp, gn in enumerate(W2GRP):
                        if gi >= 2:
                            g.wait_ge(SEM["mm"], MMgrp(gi - 2))
                        ST_w2.inc(g.dma_start(
                            out=w2buf[:, (gi % 2) * 11 * 256:
                                      (gi % 2) * 11 * 256 + gn * 256].rearrange(
                                "p (a b) -> p a b", a=gn),
                            in_=w2t_in[dot, h0:h0 + gn].rearrange(
                                "a p b -> p a b")))
                        h0 += gn
                        gi += 1

            @block.sync
            def _(sy):
                ST_g5.wait_all(sy)
                for i, (h0, hn) in enumerate([(0, 22), (22, 22), (44, 22), (66, 20)]):
                    inc(sy.dma_start(
                        out=gbuf[:, h0 * ROWS:(h0 + hn) * ROWS].rearrange(
                            "p (a b) -> p a b", a=hn),
                        in_=g_dram[h0:h0 + hn].rearrange("a p b -> p a b")),
                        "pl", 16)
                for dot in range(NDOT):
                    ST_s2w.wait_done(sy, dot - 1)
                    for qt in range(NRT):
                        ST_s1q.inc(sy.dma_start(
                            out=s1qb[:, qt * 256:qt * 256 + 256],
                            in_=s1_dram[qt * 128:qt * 128 + 128,
                                        dot * 256:dot * 256 + 256]))
                    for qt in range(NRT):
                        sy.wait_ge(SEM["ev"], EVst2(dot, qt))
                        ST_s2w.inc(sy.dma_start(
                            out=s2_dram[qt * 128:qt * 128 + 128,
                                        dot * 256:dot * 256 + 256],
                            in_=s1qb[:, qt * 256:qt * 256 + 256]))

            with nc.psum_tensor("pt2", [128, 1024], F32) as pt2:
                @block.tensor
                def _(t):
                    t.wait_ge(SEM["pl"], pl0 + 16 * 4)
                    for half in range(2):
                        for ht in range(NHT):
                            ins = t.matmul(
                                pt2[0:RANK, half * 512:half * 512 + 512],
                                a2t_sb[:, ht * RANK:ht * RANK + RANK],
                                gbuf[:, ht * ROWS + half * 512:
                                      ht * ROWS + half * 512 + 512],
                                start=(ht == 0), stop=(ht == NHT - 1),
                                skip_group_check=True)
                    inc(ins, "mm", 1)

                @block.vector
                def _(v):
                    v.wait_ge(SEM["mm"], mm0 + 1)
                    inc(v.tensor_scalar_mul(t2pad[0:RANK, :], pt2[0:RANK, :], 2.0),
                        "ev", 1)

            with nc.psum_tensor("pu", [128, 8 * 512], F32) as pu:
                @block.tensor
                def _(t):
                    t.wait_ge(SEM["ev"], ev0 + 2)
                    gi = 0
                    for dot in range(NDOT):
                        h0 = 0
                        for grp, gn in enumerate(W2GRP):
                            ST_w2.wait_done(t, gi)
                            wb = (gi % 2) * 11 * 256
                            for hl in range(gn):
                                ht = h0 + hl
                                for qt in range(8):
                                    if ht == 0 and dot > 0:
                                        t.wait_ge(SEM["ev"], EVs2(dot - 1, qt))
                                    ins = t.matmul(
                                        pu[:, qt * 512:qt * 512 + 256],
                                        gbuf[:, ht * ROWS + qt * 128:
                                              ht * ROWS + qt * 128 + 128],
                                        w2buf[:, wb + hl * 256:wb + hl * 256 + 256],
                                        start=(ht == 0), stop=False,
                                        skip_group_check=True)
                            h0 += gn
                            gi += 1
                            inc(ins, "mm", 1)
                        ST_b2.wait_done(t, dot)
                        for qt in range(8):
                            inc(t.matmul(pu[:, qt * 512:qt * 512 + 256],
                                         t2pad[:, qt * 128:qt * 128 + 128],
                                         b2pad[:, (dot % 2) * 256:
                                               (dot % 2) * 256 + 256],
                                         start=False, stop=True,
                                         skip_group_check=True), "mm", 1)

                @block.vector
                def _(v):
                    for dot in range(NDOT):
                        ST_s1q.wait_done(v, dot)
                        for qt in range(8):
                            sl = qt * 256
                            inc(v.tensor_scalar(s1qb[:, sl:sl + 256],
                                                s1qb[:, sl:sl + 256],
                                                m1r1[:, qt:qt + 1],
                                                m1r1[:, NRT + qt:NRT + qt + 1],
                                                ALU.subtract, ALU.mult), "ev", 1)
                        for qt in range(8):
                            sl = qt * 256
                            v.wait_ge(SEM["mm"], MMq(dot, qt))
                            inc(v.tensor_add(s1qb[:, sl:sl + 256],
                                             pu[:, qt * 512:qt * 512 + 256],
                                             s1qb[:, sl:sl + 256]), "ev", 1)
                            v.reduce_sum(lnt1[:, qt:qt + 1], s1qb[:, sl:sl + 256],
                                         AXX)
                            v.tensor_mul(sqs[:, 0:256], s1qb[:, sl:sl + 256],
                                         s1qb[:, sl:sl + 256])
                            v.reduce_sum(lnt2[:, qt:qt + 1], sqs[:, 0:256], AXX)
                            v.tensor_add(acc2[:, qt:qt + 1], acc2[:, qt:qt + 1],
                                         lnt1[:, qt:qt + 1])
                            inc(v.tensor_add(acc2[:, NRT + qt:NRT + qt + 1],
                                             acc2[:, NRT + qt:NRT + qt + 1],
                                             lnt2[:, qt:qt + 1]), "ev", 1)

                @block.scalar
                def _(s):
                    for dot in range(NDOT):
                        for qt in range(8):
                            s.wait_ge(SEM["ev"], EVs2(dot, qt))
                            sl = qt * 256
                            inc(s.activation(sqs[:, 0:256], s1qb[:, sl:sl + 256],
                                             AF.Square,
                                             accum_out=lnt2[:, qt:qt + 1]),
                                "sc", 1)

        fence()
        if dbg_stage == "s2":
            dbg_dump([(s2_dram.rearrange("(r p) c -> p r c", p=128), 8 * H)])
        if dbg_stage == "ln2":
            dbg_dump([(acc2[:, :], 16)], wait=("ev", C["ev"]))

        ev0, sc0 = C["ev"], C["sc"]
        ST_s7 = _PS("s7_s", 1)
        ST_out = _PS("out_s", 1)

        @block.vector
        def _(v):
            v.tensor_scalar_mul(m2r2[:, 0:NRT], acc2[:, 0:NRT], 1.0 / H)
            v.tensor_scalar_mul(lnt1[:, :], acc2[:, NRT:2 * NRT], 1.0 / H)
            v.tensor_mul(lnt2[:, :], m2r2[:, 0:NRT], m2r2[:, 0:NRT])
            v.tensor_sub(lnt1[:, :], lnt1[:, :], lnt2[:, :])
            inc(v.tensor_scalar_add(lnt1[:, :], lnt1[:, :], EPS), "ev", 1)
            v.wait_ge(SEM["sc"], sc0 + 1)
            inc(v.reciprocal(m2r2[:, NRT:2 * NRT], lnt2[:, :]), "ev", 1)

        @block.scalar
        def _(s):
            s.wait_ge(SEM["ev"], ev0 + 1)
            inc(s.activation(lnt2[:, :], lnt1[:, :], AF.Sqrt), "sc", 1)

        with (
            nc.sbuf_tensor("srow7", [128, 2 * H], F32) as srow7,
            nc.sbuf_tensor("orow", [128, 2 * H], F32) as orow,
        ):
            @block.sync
            def _(sy):
                ST_s2w.wait_all(sy)
                for rt in range(NRT + 2):
                    if rt < NRT:
                        if rt >= 2:
                            sy.wait_ge(SEM["ev"], ev0 + 2 + rt - 1)
                        ST_s7.inc(sy.dma_start(
                            out=srow7[:, (rt % 2) * H:(rt % 2) * H + H],
                            in_=s2_dram[rt * 128:rt * 128 + 128, :]))
                    if rt >= 2:
                        ot = rt - 2
                        sy.wait_ge(SEM["ev"], ev0 + 2 + ot + 1)
                        ST_out.inc(sy.dma_start(
                            out=out_ap[ot * 128:ot * 128 + 128, :],
                            in_=orow[:, (ot % 2) * H:(ot % 2) * H + H]))
                ST_out.wait_all(sy)

            @block.vector
            def _(v):
                v.wait_ge(SEM["ev"], ev0 + 2)
                for rt in range(NRT):
                    ST_s7.wait_done(v, rt)
                    ST_out.wait_done(v, rt - 2)
                    sl = (rt % 2) * H
                    inc(v.tensor_scalar(orow[:, sl:sl + H], srow7[:, sl:sl + H],
                                        m2r2[:, rt:rt + 1],
                                        m2r2[:, NRT + rt:NRT + rt + 1],
                                        ALU.subtract, ALU.mult), "ev", 1)

    except _DbgFin:
        pass
    es.close()
    return nc



def _stage(inputs):
    x = np.asarray(inputs["L0_attn_hidden_states"], dtype=np.float32)
    wqkv = np.asarray(inputs["L0_attn_w_qkv"], dtype=np.float32)
    wo = np.asarray(inputs["L0_attn_w_o"], dtype=np.float32)
    w1 = np.asarray(inputs["L0_ffn_w1"], dtype=np.float32)
    w3 = np.asarray(inputs["L0_ffn_w3"], dtype=np.float32)
    w2 = np.asarray(inputs["L0_ffn_w2"], dtype=np.float32)
    a1 = np.asarray(inputs["L0_ffn_lora_A1"], dtype=np.float32)
    b1 = np.asarray(inputs["L0_ffn_lora_B1"], dtype=np.float32)
    a3 = np.asarray(inputs["L0_ffn_lora_A3"], dtype=np.float32)
    b3 = np.asarray(inputs["L0_ffn_lora_B3"], dtype=np.float32)
    a2 = np.asarray(inputs["L0_ffn_lora_A2"], dtype=np.float32)
    b2 = np.asarray(inputs["L0_ffn_lora_B2"], dtype=np.float32)

    cc = np.ascontiguousarray
    shared = {
        "wq": cc(wqkv[:, :4096].reshape(32, 128, 32, 128).transpose(2, 1, 0, 3)),
        "wk": cc(wqkv[:, 4096:5120].reshape(32, 128, 8, 128).transpose(2, 1, 0, 3)),
        "wv": cc(wqkv[:, 5120:6144].reshape(32, 128, 1024)),
        "wo": cc(wo.reshape(32, 128, 8, 512).transpose(2, 1, 0, 3)),
        "w1t": cc(w1.reshape(NHT, 128, 32, 128).transpose(0, 3, 2, 1)),
        "w3t": cc(w3.reshape(NHT, 128, 32, 128).transpose(0, 3, 2, 1)),
        "w2t": cc(w2.reshape(16, 256, NHT, 128).transpose(0, 2, 3, 1)),
        "a1t": cc(a1.reshape(RANK, 32, 128).transpose(2, 1, 0)).reshape(128, 32 * RANK),
        "a3t": cc(a3.reshape(RANK, 32, 128).transpose(2, 1, 0)).reshape(128, 32 * RANK),
        "a2t": cc(a2.reshape(RANK, NHT, 128).transpose(2, 1, 0)).reshape(128, NHT * RANK),
        "b1t": cc(b1.T),
        "b3t": cc(b3.T),
        "b2t": cc(b2.T),
        "ident": np.eye(128, dtype=np.float32),
    }
    xt_b = [cc(x[b].reshape(8, 512, 32, 128).transpose(0, 3, 2, 1))
            for b in range(B)]

    qq = np.arange(512, dtype=np.float32)
    kk = np.arange(128, dtype=np.float32)
    in_maps = []
    for c in range(8):
        b, j = divmod(c, 4)
        chunks = (j, 7 - j)
        rel = np.empty((sum(NKT), 128, 512), dtype=np.float32)
        s = 0
        for qc, ch in enumerate(chunks):
            qbase = 512.0 * ch
            for kt in range(NKT[qc]):
                rel[s] = (qbase + qq)[None, :] - (kt * 128 + kk)[:, None]
                s += 1
        m = dict(shared)
        m["xt"] = xt_b[b]
        m["xto"] = cc(np.stack([xt_b[b][ch] for ch in chunks]))
        m["xown"] = cc(np.concatenate(
            [x[b, 512 * ch:512 * ch + 512] for ch in chunks]))
        m["rel"] = rel
        in_maps.append(m)
    return in_maps


_NC_CACHE = {}


def kernel(**inputs):
    stage = _DBG["stage"]
    if stage not in _NC_CACHE:
        _NC_CACHE[stage] = build_nc(stage)
    nc = _NC_CACHE[stage]
    in_maps = _stage(inputs)
    res = run_bass_kernel_spmd(nc, in_maps, core_ids=list(range(8)))
    if stage is not None:
        return [r["dbg"] for r in res.results]
    out = np.empty((B, S, H), dtype=np.float32)
    for c in range(8):
        b, j = divmod(c, 4)
        r = res.results[c]["out"]
        out[b, 512 * j:512 * j + 512] = r[0:512]
        out[b, 512 * (7 - j):512 * (7 - j) + 512] = r[512:1024]
    return out


if __name__ == "__main__":
    n = build_nc()
    print("build ok")
